# version 05 (final)
# baseline (speedup 1.0000x reference)
import os
import numpy as np
from contextlib import ExitStack
import concourse.bass as bass
import concourse.mybir as mybir
from concourse.bass_utils import run_bass_kernel_spmd

F32 = mybir.dt.float32
ALU = mybir.AluOpType
AF = mybir.ActivationFunctionType
AX = mybir.AxisListType

NCORES = 8
B_C = 2
L = 2048
D = 1024
E = 2048
NST = 16
T = B_C * L
QM = 512
QE = 256
NEXP = 16
NH = 16
HD = 64
EPS = 1e-5

_DBG_STOP = int(os.environ.get("MK_STOP", "0"))


class Ctx:

    def __init__(self, nc, cur, eng, sems):
        self.nc, self.cur, self.eng = nc, cur, eng
        self.sems = sems
        self.cnt = {k: 0 for k in sems}

    def _fin(self, instr, inc):
        if inc is not None:
            sem, k = inc
            self.cnt[sem] += k
            if instr is not None:
                instr.then_inc(self.sems[sem], k)
        return instr

    def wait(self, engine, sem, val=None):
        v = self.cnt[sem] if val is None else val
        if self.cur == engine and v > 0:
            self.eng.wait_ge(self.sems[sem], v)

    def dma(self, q, dst, src, sem=None, slow=False):
        sem = sem or ("dG" if q == "gpsimd" else "dW")
        i = None
        if self.cur == q:
            i = self.eng.dma_start(out=dst, in_=src,
                                   allow_slow_non_contiguous=slow)
        return self._fin(i, (sem, 16))

    def mm(self, out, lhsT, rhs, start=True, stop=True, inc=None, tp=None):
        i = None
        if self.cur == "tensor":
            i = self.eng.matmul(out, lhsT, rhs, start=start, stop=stop,
                                tile_position=tp)
        return self._fin(i, inc)

    def tr(self, out, in_, ident, inc=None):
        i = self.eng.transpose(out, in_, ident) if self.cur == "tensor" else None
        return self._fin(i, inc)

    def v(self, op, *a, inc=None, **kw):
        i = getattr(self.eng, op)(*a, **kw) if self.cur == "vector" else None
        return self._fin(i, inc)

    def s(self, op, *a, inc=None, **kw):
        i = getattr(self.eng, op)(*a, **kw) if self.cur == "scalar" else None
        return self._fin(i, inc)

    def g(self, op, *a, inc=None, **kw):
        i = getattr(self.eng, op)(*a, **kw) if self.cur == "gpsimd" else None
        return self._fin(i, inc)


class Arena:
    def __init__(self, ap, cols):
        self.ap, self.cols = ap, cols
        self.off = 0
        self.hi = 0

    def alloc(self, n, name=""):
        a = self.ap[:, self.off:self.off + n]
        self.off += n
        self.hi = max(self.hi, self.off)
        assert self.off <= self.cols, f"SBUF arena overflow at {name}: {self.off}"
        return a

    def mark(self):
        return self.off

    def reset(self, m):
        self.off = m


def build_program():
    nc = bass.Bass("TRN2", target_bir_lowering=False, debug=False, dynamic_dma_scratch_size=1024)
    dram = {}

    def din(name, shape):
        dram[name] = nc.dram_tensor(name, shape, F32, kind="ExternalInput").ap()
        return dram[name]

    def dint(name, shape):
        dram[name] = nc.dram_tensor(name, shape, F32, kind="Internal").ap()
        return dram[name]

    din("x_t", [D, T])
    din("c0_t", [D, B_C]); din("wmod0P", [48, 128, 1024])
    din("convwP", [16, 128, 3072]); din("convb", [E, 1])
    din("inwP", [16, 128, 2048]); din("inb", [E, 1])
    din("dtwP", [16, 128, 2048]); din("dtb", [E, 1])
    din("bpwP", [128, 256]); din("bpb", [NST, 1])
    din("cpwP", [128, 256]); din("cpb", [NST, 1])
    din("alog", [E, NST]); din("dskip", [E, 1])
    din("outwP", [8, 128, 2048]); din("outb", [D, 1])
    din("rtr", [D, NEXP])
    din("guw", [NEXP, D, 1024]); din("gub", [NEXP, 1024])
    din("dnw", [NEXP, 512, D]); din("dnb", [NEXP, D])
    din("c1_t", [D, B_C]); din("wmod1P", [48, 128, 1024])
    din("wqp", [D, D])
    din("wkT", [192, D]); din("mkpos", [192, D]); din("mkneg", [192, D])
    din("wvT", [192, D]); din("vdm", [192, D])
    din("wproj", [D, D])
    din("g1rT", [4, 1024]); din("g2r", [4, 256])
    din("g1iT", [4, 1024]); din("g2i", [4, 256])
    din("fb", [D, 1])
    din("ropec", [128, L]); din("ropes2", [128, L])
    din("pswapc", [128, 1])

    out_t = nc.dram_tensor("out_t", [D, T], F32, kind="ExternalOutput").ap()

    dint("xres", [D, T])
    dint("qTs", [D, L]); dint("kTs", [D, L]); dint("vtok", [L, D]); dint("yTs", [D, L])
    dint("wkeff", [D, D]); dint("wveff", [D, D])
    dint("wcomb", [D, D])

    es = ExitStack()
    COLS = 53000
    arena_t = es.enter_context(nc.sbuf_tensor("arena", [128, COLS], F32))
    ps = [es.enter_context(nc.psum_tensor(f"ps{i}", [128, 512], F32)) for i in range(8)]
    sems = {k: es.enter_context(nc.semaphore(k))
            for k in ["dW", "dG", "tc", "vc", "sc", "gc"]}
    block = es.enter_context(nc.Block())

    def emit_all(ctx):
        A = Arena(arena_t[:], COLS)
        al = A.alloc

        ones = al(128, "ones")
        identSB = al(128, "ident")
        mask128 = al(128, "mask")
        k16 = al(NEXP * 128, "k16")
        colj = al(128, "colj")
        rowi = al(4, "rowi")
        mod0 = al(6 * 8 * B_C, "mod0")
        mod1 = al(6 * 8 * B_C, "mod1")
        a_sb = al(NST * NST, "a_sb")
        inb_sb = al(16, "inb"); dtb_sb = al(16, "dtb"); convb_sb = al(16, "convb")
        outb_sb = al(8, "outb"); dskip_sb = al(16, "dskip")
        bcb_sb = al(2, "bcb")
        fb_sb = al(8, "fb")
        hstate = al(NST * NST, "hstate")
        rows_sc = al(3 * 1344, "rows")
        cst = al(16, "cst")

        def modcol(mreg, k, j, b):
            c = (k * 8 + j) * B_C + b
            return mreg[:, c:c + 1]

        ctx.v("memset", ones[:], 1.0)
        ctx.v("memset", cst[:, 0:1], float(EPS))
        ctx.v("memset", cst[:, 1:2], -0.5)
        ctx.v("memset", cst[:, 2:3], 0.125)
        ctx.v("memset", cst[:, 3:4], 1.702)
        ctx.v("memset", cst[:, 4:5], -1.0)
        ctx.g("iota", colj[:], [[1, 128]], base=0, channel_multiplier=0,
              allow_small_or_imprecise_dtypes=True, inc=("gc", 1))
        ctx.g("iota", rowi[:, 0:1], [[0, 1]], base=0, channel_multiplier=1,
              allow_small_or_imprecise_dtypes=True, inc=("gc", 1))
        ctx.g("iota", k16[0:16, :], [[1, NEXP], [0, 128]], base=0,
              channel_multiplier=0, allow_small_or_imprecise_dtypes=True,
              inc=("gc", 1))
        ctx.wait("vector", "gc")
        ctx.v("tensor_scalar", identSB[:], colj[:], rowi[:, 0:1], None, ALU.is_equal)
        ctx.v("tensor_scalar", mask128[:], colj[:], rowi[:, 0:1], -1e30,
              ALU.is_gt, ALU.mult)
        ctx.v("tensor_scalar", k16[0:16, :], k16[0:16, :], rowi[0:16, 0:1], None,
              ALU.is_equal, inc=("vc", 1))

        for nm, reg, nt in [("inb", inb_sb, 16), ("dtb", dtb_sb, 16),
                            ("convb", convb_sb, 16), ("dskip", dskip_sb, 16),
                            ("outb", outb_sb, 8), ("fb", fb_sb, 8)]:
            ctx.dma("sync", reg[:, 0:nt].rearrange("p (a o) -> p a o", o=1),
                    dram[nm].rearrange("(a p) o -> p a o", p=128), slow=True)
        ctx.dma("sync", bcb_sb[0:16, 0:1], dram["bpb"])
        ctx.dma("sync", bcb_sb[0:16, 1:2], dram["cpb"])
        ctx.dma("sync", a_sb[:, 0:NST * NST].rearrange("p (a n) -> p a n", n=NST),
                dram["alog"].rearrange("(a p) n -> p a n", p=128), slow=True)
        ctx.wait("scalar", "dW")
        ctx.s("activation", a_sb[:, 0:NST * NST], a_sb[:, 0:NST * NST], AF.Exp,
              inc=("sc", 1))
        ctx.wait("vector", "sc")
        ctx.v("tensor_scalar_mul", a_sb[:, 0:NST * NST], a_sb[:, 0:NST * NST], -1.0,
              inc=("vc", 1))

        def adaln_mod(cname, wname, mreg):
            m0 = A.mark()
            csil = al(B_C * 8, "csil")
            wreg = al(8 * 128, "wmodslab")
            ctx.wait("sync", "tc")
            ctx.dma("sync", csil[:, 0:B_C * 8].rearrange("p (a b) -> p a b", b=B_C),
                    dram[cname].rearrange("(a p) b -> p a b", p=128), slow=True)
            ctx.wait("scalar", "dW")
            ctx.s("activation", csil[:, 0:B_C * 8], csil[:, 0:B_C * 8], AF.Silu,
                  inc=("sc", 1))
            ctx.wait("tensor", "sc")
            wfree = ("tc", ctx.cnt["tc"])
            for ft in range(48):
                ctx.wait("sync", *wfree)
                ctx.dma("sync", wreg[:, 0:1024], dram[wname][ft])
                bank = ps[ft % 2]
                ctx.wait("tensor", "dW")
                ctx.wait("tensor", "vc")
                for dt in range(8):
                    ctx.mm(bank[:, 0:B_C], wreg[:, dt * 128:(dt + 1) * 128],
                           csil[:, dt * B_C:(dt + 1) * B_C],
                           start=(dt == 0), stop=(dt == 7),
                           inc=("tc", 1) if dt == 7 else None)
                wfree = ("tc", ctx.cnt["tc"])
                ctx.wait("vector", "tc")
                ctx.v("tensor_copy", mreg[:, ft * B_C:(ft + 1) * B_C],
                      bank[:, 0:B_C], inc=("vc", 1))
            nn = 8 * B_C
            sgn = al(2 * nn, "sgn"); ab = al(2 * nn, "ab")
            ctx.wait("scalar", "vc")
            for i, k in enumerate((2, 5)):
                cols = mreg[:, (k * 8) * B_C:(k * 8 + 8) * B_C]
                ctx.s("sign", sgn[:, i * nn:(i + 1) * nn], cols, inc=("sc", 1))
                ctx.s("activation", ab[:, i * nn:(i + 1) * nn], cols, AF.Abs,
                      inc=("sc", 1))
            ctx.wait("vector", "sc")
            for i in range(2):
                ctx.v("tensor_scalar_max", ab[:, i * nn:(i + 1) * nn],
                      ab[:, i * nn:(i + 1) * nn], 0.25)
            for k in (1, 4):
                cols = mreg[:, (k * 8) * B_C:(k * 8 + 8) * B_C]
                ctx.v("tensor_scalar_add", cols, cols, 1.0, inc=("vc", 1))
            for i, k in enumerate((2, 5)):
                cols = mreg[:, (k * 8) * B_C:(k * 8 + 8) * B_C]
                ctx.v("tensor_tensor", cols, sgn[:, i * nn:(i + 1) * nn],
                      ab[:, i * nn:(i + 1) * nn], ALU.mult, inc=("vc", 1))
            A.reset(m0)

        def ln_mod(xt, ht, sqt, w, mreg, kc, ks, b):
            groups = [(0, min(512, w))]
            if w > 512:
                groups.append((w - 512, 512))
            banks = {0: (ps[6], ps[7])}
            if len(groups) > 1:
                banks[1] = (ps[4], ps[5])
            for j in range(8):
                ctx.s("activation", sqt[j], xt[j], AF.Square,
                      inc=("sc", 1) if j == 7 else None)
            ctx.wait("tensor", "sc")
            for gi, (c0, wid) in enumerate(groups):
                bs, bq = banks[gi]
                for j in range(8):
                    ctx.mm(bs[0:1, 0:wid], ones[:, 0:1], xt[j][:, c0:c0 + wid],
                           start=(j == 0), stop=(j == 7),
                           inc=("tc", 1) if j == 7 else None)
                for j in range(8):
                    ctx.mm(bq[0:1, 0:wid], ones[:, 0:1], sqt[j][:, c0:c0 + wid],
                           start=(j == 0), stop=(j == 7),
                           inc=("tc", 1) if j == 7 else None)
            ctx.wait("vector", "tc")
            mu = rows_sc[0:1, 0:w]
            var = rows_sc[0:1, 1344:1344 + w]
            alp = rows_sc[0:1, 2688:2688 + w]
            for gi, (c0, wid) in enumerate(groups):
                bs, bq = banks[gi]
                ctx.v("tensor_scalar_mul", mu[:, c0:c0 + wid], bs[0:1, 0:wid],
                      1.0 / D)
                ctx.v("tensor_scalar_mul", var[:, c0:c0 + wid], bq[0:1, 0:wid],
                      1.0 / D)
            ctx.v("tensor_mul", alp, mu, mu)
            ctx.v("tensor_sub", var, var, alp, inc=("vc", 1))
            ctx.wait("scalar", "vc")
            ctx.s("activation", alp, var, AF.Ln, bias=cst[0:1, 0:1], inc=("sc", 1))
            ctx.s("activation", alp, alp, AF.Exp, scale=cst[0:1, 1:2], inc=("sc", 1))
            ctx.wait("vector", "sc")
            nma = var
            ctx.v("scalar_tensor_tensor", nma, mu, -1.0, alp, ALU.mult, ALU.mult,
                  inc=("vc", 1))
            ctx.wait("tensor", "vc")
            for gi, (c0, wid) in enumerate(groups):
                bs, bq = banks[gi]
                ctx.mm(bs[:, 0:wid], ones[0:1, 0:128], alp[:, c0:c0 + wid],
                       inc=("tc", 1))
                ctx.mm(bq[:, 0:wid], ones[0:1, 0:128], nma[:, c0:c0 + wid],
                       inc=("tc", 1))
            ctx.wait("vector", "tc")
            for j in range(8):
                for gi, (c0, wid) in enumerate(groups):
                    bs, bq = banks[gi]
                    ctx.v("tensor_mul", ht[j][:, c0:c0 + wid], xt[j][:, c0:c0 + wid],
                          bs[:, 0:wid])
                    ctx.v("tensor_add", ht[j][:, c0:c0 + wid], ht[j][:, c0:c0 + wid],
                          bq[:, 0:wid])
                ctx.v("tensor_scalar", ht[j], ht[j], modcol(mreg, kc, j, b),
                      modcol(mreg, ks, j, b), ALU.mult, ALU.add,
                      inc=("vc", 1) if j == 7 else None)
            ctx.v("engine_nop", inc=("vc", 1))

        adaln_mod("c0_t", "wmod0P", mod0)
        adaln_mod("c1_t", "wmod1P", mod1)

        mW = A.mark()
        ctx.wait("sync", "tc")
        ctx.wait("sync", "vc")
        ctx.wait("sync", "sc")
        wk_sb = [al(1024, "wk0"), al(1024, "wk1")]
        mp_sb = [al(1024, "mp0"), al(1024, "mp1")]
        mn_sb = [al(1024, "mn0"), al(1024, "mn1")]
        wv_sb = [al(1024, "wv0"), al(1024, "wv1")]
        vd_sb = [al(1024, "vd0"), al(1024, "vd1")]
        evW = al(512, "evW")
        g1_sb = al(1024, "g1W"); g1i_sb = al(1024, "g1iW")
        g2_sb = al(256, "g2W"); g2i_sb = al(256, "g2iW")
        zR = al(256, "zRW"); zI = al(256, "zIW"); zN = al(256, "zNW")

        for nm, dsts in [("wkT", wk_sb), ("mkpos", mp_sb), ("mkneg", mn_sb),
                         ("wvT", wv_sb), ("vdm", vd_sb)]:
            ctx.dma("sync", dsts[0][:, 0:1024], dram[nm][0:128, :])
            ctx.dma("sync", dsts[1][0:64, 0:1024], dram[nm][128:192, :])
        ctx.dma("sync", g1_sb[0:4, 0:1024], dram["g1rT"])
        ctx.dma("sync", g1i_sb[0:4, 0:1024], dram["g1iT"])
        ctx.dma("sync", g2_sb[0:4, 0:256], dram["g2r"])
        ctx.dma("sync", g2i_sb[0:4, 0:256], dram["g2i"])
        ctx.wait("vector", "dW")
        ctx.v("tensor_scalar_mul", mn_sb[0][:, 0:1024], mn_sb[0][:, 0:1024], -1.0)
        ctx.v("tensor_scalar_mul", mn_sb[1][0:64, 0:1024], mn_sb[1][0:64, 0:1024],
              -1.0, inc=("vc", 1))
        for dst, wsrc, msrcs in [("wkeff", wk_sb, [mp_sb, mn_sb]),
                                 ("wveff", wv_sb, [vd_sb])]:
            for it in range(8):
                for jh in range(2):
                    bank = ps[(it * 2 + jh) % 2]
                    ctx.wait("tensor", "vc")
                    nmm = len(msrcs) * 2
                    i = 0
                    for msb in msrcs:
                        for rt in range(2):
                            rr = 128 if rt == 0 else 64
                            i += 1
                            ctx.mm(bank[0:128, 0:512],
                                   wsrc[rt][0:rr, it * 128:(it + 1) * 128],
                                   msb[rt][0:rr, jh * 512:(jh + 1) * 512],
                                   start=(i == 1), stop=(i == nmm),
                                   inc=("tc", 1) if i == nmm else None)
                    ctx.wait("vector", "tc")
                    ctx.wait("vector", "dG")
                    ctx.v("tensor_copy", evW[:, 0:512], bank[:, 0:512],
                          inc=("vc", 1))
                    ctx.wait("sync", "vc")
                    ctx.dma("sync", dram[dst][it * 128:(it + 1) * 128,
                                              jh * 512:(jh + 1) * 512],
                            evW[:, 0:512], sem="dG")
        wcombV = dram["wcomb"].rearrange("k (h b e) -> k h b e", b=32, e=16)
        for kt in range(8):
            ctx.wait("tensor", "vc")
            ctx.mm(ps[2][:, 0:256], g1_sb[0:4, kt * 128:(kt + 1) * 128],
                   g2_sb[0:4, 0:256], inc=("tc", 1))
            ctx.mm(ps[3][:, 0:256], g1i_sb[0:4, kt * 128:(kt + 1) * 128],
                   g2i_sb[0:4, 0:256], inc=("tc", 1))
            ctx.wait("vector", "tc")
            ctx.wait("vector", "dG")
            ctx.v("tensor_copy", zR[:, 0:256], ps[2][:, 0:256])
            ctx.v("tensor_copy", zI[:, 0:256], ps[3][:, 0:256])
            ctx.v("tensor_scalar_mul", zN[:, 0:256], ps[3][:, 0:256], -1.0,
                  inc=("vc", 1))
            ctx.wait("sync", "vc")
            for dloc in range(4):
                for g in range(16):
                    r = (kt * 4 + dloc) * 16 + g
                    for src, row, hcol in [(zR, r, 0), (zR, 512 + r, 1),
                                           (zI, r, 1), (zN, 512 + r, 0)]:
                        ctx.dma("sync", wcombV[row, hcol],
                                src[dloc * 32:(dloc + 1) * 32,
                                    g * 16:(g + 1) * 16], sem="dG", slow=True)
        A.reset(mW)
        if _DBG_STOP == 45:
            for j in range(8):
                ctx.dma("sync", out_t[j * 128:(j + 1) * 128, 0:1024],
                        dram["wkeff"][j * 128:(j + 1) * 128, :], sem="dG")
                ctx.dma("sync", out_t[j * 128:(j + 1) * 128, 1024:2048],
                        dram["wveff"][j * 128:(j + 1) * 128, :], sem="dG")
                ctx.dma("sync", out_t[j * 128:(j + 1) * 128, 2048:3072],
                        dram["wcomb"][j * 128:(j + 1) * 128, :], sem="dG")
            ctx.wait("gpsimd", "dW")
            ctx.wait("gpsimd", "dG")
            return

        m_ph2 = A.mark()
        xbuf = [al(QM + 2, f"x{j}") for j in range(8)]
        hbuf = [al(QM + 2, f"h{j}") for j in range(8)]
        xc = [al(QM, f"xc{j}") for j in range(16)]
        ubuf = [al(QM, f"u{j}") for j in range(16)]
        dbuf = [al(QM, f"d{j}") for j in range(16)]
        btct = al(2 * QM, "btct")
        scan_sc = al(3 * QM, "scansc")
        wconv_r = al(2 * 24 * 128, "wconv")
        wconv = [wconv_r[:, 0:3072], wconv_r[:, 3072:6144]]
        wbig = [al(16 * 128, "wbig0"), al(16 * 128, "wbig1")]
        dubuf = xc
        ybuf = ([hbuf[j][:, 0:QM] for j in range(8)] +
                [wconv_r[:, i * QM:(i + 1) * QM] for i in range(8)])

        wconv_free = [("tc", 0), ("tc", 0)]
        wbig_free = [("tc", 0), ("tc", 0)]

        for b in range(B_C):
            ctx.v("memset", hstate[:, 0:NST * NST], 0.0, inc=("vc", 1))
            for ci in range(L // QM):
                base = b * L + ci * QM
                w = QM + 2
                lo = base - 1
                src_lo = max(lo, b * L)
                src_hi = min(base + QM + 1, (b + 1) * L)
                d0 = src_lo - lo
                ctx.wait("sync", "vc")
                ctx.wait("vector", "dG")
                if d0 > 0:
                    for j in range(8):
                        ctx.v("memset", xbuf[j][:, 0:d0], 0.0)
                if base + QM + 1 > src_hi:
                    for j in range(8):
                        ctx.v("memset", xbuf[j][:, src_hi - lo:w], 0.0)
                ctx.v("engine_nop", inc=("vc", 1))
                ctx.wait("sync", "vc")
                for j in range(8):
                    ctx.dma("sync", xbuf[j][:, d0:src_hi - lo],
                            dram["x_t"][j * 128:(j + 1) * 128, src_lo:src_hi])
                ctx.wait("scalar", "dW")
                ctx.wait("scalar", "vc")
                ctx.wait("scalar", "dG")
                ctx.wait("scalar", "tc")
                ctx.wait("vector", "dW")
                ln_mod([x[:, 0:w] for x in xbuf], [h[:, 0:w] for h in hbuf],
                       [h[:, 0:w] for h in hbuf], w, mod0, 1, 0, b)
                if d0 > 0 or base + QM + 1 > src_hi:
                    for j in range(8):
                        if d0 > 0:
                            ctx.v("memset", hbuf[j][:, 0:d0], 0.0)
                        if base + QM + 1 > src_hi:
                            ctx.v("memset", hbuf[j][:, src_hi - lo:w], 0.0)
                    ctx.v("engine_nop", inc=("vc", 1))
                if _DBG_STOP == 21:
                    ctx.wait("sync", "vc")
                    for j in range(8):
                        ctx.dma("sync", out_t[j * 128:(j + 1) * 128, base:base + QM],
                                hbuf[j][:, 1:1 + QM], sem="dG")
                if _DBG_STOP == 27:
                    ctx.wait("sync", "vc")
                    for j in range(8):
                        ctx.dma("sync", out_t[j * 128:(j + 1) * 128,
                                              base:base + 2],
                                hbuf[j][:, QM:QM + 2], sem="dG")
                        ctx.dma("sync", out_t[j * 128:(j + 1) * 128,
                                              base + 4:base + 6],
                                xbuf[j][:, QM:QM + 2], sem="dG")
                if _DBG_STOP == 26:
                    ctx.wait("sync", "vc")
                    ctx.dma("sync", out_t[0:1, base:base + QM],
                            rows_sc[0:1, 1:1 + QM], sem="dG")
                    ctx.dma("sync", out_t[1:2, base:base + QM],
                            rows_sc[0:1, 1201:1201 + QM], sem="dG")
                for ft in range(16):
                    par = ft % 2
                    ctx.wait("sync", *wconv_free[par])
                    ctx.dma("sync", wconv[par][:, 0:3072], dram["convwP"][ft])
                    bank = ps[ft % 2]
                    ctx.wait("tensor", "dW")
                    ctx.wait("tensor", "vc")
                    nmm = 0
                    for k in range(3):
                        for dt in range(8):
                            nmm += 1
                            ctx.mm(bank[:, 0:QM],
                                   wconv[par][:, k * 1024 + dt * 128:
                                              k * 1024 + (dt + 1) * 128],
                                   hbuf[dt][:, k:k + QM],
                                   start=(nmm == 1), stop=(nmm == 24),
                                   inc=("tc", 1) if nmm == 24 else None)
                    wconv_free[par] = ("tc", ctx.cnt["tc"])
                    ctx.wait("vector", "tc")
                    ctx.v("tensor_scalar_add", xc[ft][:, 0:QM], bank[:, 0:QM],
                          convb_sb[:, ft:ft + 1], inc=("vc", 1))
                if _DBG_STOP == 22:
                    ctx.wait("sync", "vc")
                    for j in range(8):
                        ctx.dma("sync", out_t[j * 128:(j + 1) * 128, base:base + QM],
                                xc[j][:, 0:QM], sem="dG")
                for ft in range(16):
                    par = ft % 2
                    ctx.wait("sync", *wbig_free[par])
                    ctx.dma("sync", wbig[par][:, 0:2048], dram["inwP"][ft])
                    bank = ps[2 + ft % 2]
                    ctx.wait("tensor", "dW")
                    ctx.wait("tensor", "vc")
                    ctx.wait("tensor", "sc")
                    for kt in range(16):
                        ctx.mm(bank[:, 0:QM],
                               wbig[par][:, kt * 128:(kt + 1) * 128],
                               xc[kt][:, 0:QM], start=(kt == 0), stop=(kt == 15),
                               inc=("tc", 1) if kt == 15 else None)
                    wbig_free[par] = ("tc", ctx.cnt["tc"])
                    ctx.wait("scalar", "tc")
                    ctx.s("activation", ubuf[ft][:, 0:QM], bank[:, 0:QM], AF.Silu,
                          bias=inb_sb[:, ft:ft + 1], inc=("sc", 1))
                if _DBG_STOP == 23:
                    ctx.wait("sync", "sc")
                    for j in range(8):
                        ctx.dma("sync", out_t[j * 128:(j + 1) * 128, base:base + QM],
                                ubuf[j][:, 0:QM], sem="dG")
                for ft in range(16):
                    par = ft % 2
                    ctx.wait("sync", *wbig_free[par])
                    ctx.dma("sync", wbig[par][:, 0:2048], dram["dtwP"][ft])
                    bank = ps[2 + ft % 2]
                    ctx.wait("tensor", "dW")
                    ctx.wait("tensor", "sc")
                    for kt in range(16):
                        ctx.mm(bank[:, 0:QM],
                               wbig[par][:, kt * 128:(kt + 1) * 128],
                               ubuf[kt][:, 0:QM], start=(kt == 0), stop=(kt == 15),
                               inc=("tc", 1) if kt == 15 else None)
                    wbig_free[par] = ("tc", ctx.cnt["tc"])
                    ctx.wait("scalar", "tc")
                    ctx.s("activation", dbuf[ft][:, 0:QM], bank[:, 0:QM], AF.Exp,
                          bias=dtb_sb[:, ft:ft + 1], inc=("sc", 1))
                ctx.wait("vector", "sc")
                for ft in range(16):
                    ctx.v("tensor_scalar_add", dbuf[ft][:, 0:QM],
                          dbuf[ft][:, 0:QM], 1.0,
                          inc=("vc", 1) if ft == 15 else None)
                ctx.wait("scalar", "vc")
                for ft in range(16):
                    ctx.s("activation", dbuf[ft][:, 0:QM], dbuf[ft][:, 0:QM],
                          AF.Ln, inc=("sc", 1) if ft == 15 else None)
                if _DBG_STOP == 24:
                    ctx.wait("sync", "sc")
                    for j in range(8):
                        ctx.dma("sync", out_t[j * 128:(j + 1) * 128, base:base + QM],
                                dbuf[j][:, 0:QM], sem="dG")
                ctx.wait("tensor", "sc")
                for (wname, bank, bcol, par) in [("bpwP", ps[4], 0, 0),
                                                 ("cpwP", ps[5], 1, 1)]:
                    ctx.wait("sync", *wbig_free[par])
                    ctx.dma("sync", wbig[par][:, 0:256], dram[wname])
                    ctx.wait("tensor", "dW")
                    for kt in range(16):
                        ctx.mm(bank[0:NST, 0:QM],
                               wbig[par][:, kt * NST:(kt + 1) * NST],
                               ubuf[kt][:, 0:QM], start=(kt == 0), stop=(kt == 15),
                               inc=("tc", 1) if kt == 15 else None)
                    wbig_free[par] = ("tc", ctx.cnt["tc"])
                    ctx.wait("vector", "tc")
                    ctx.v("tensor_scalar_add",
                          btct[0:NST, bcol * QM:(bcol + 1) * QM],
                          bank[0:NST, 0:QM], bcb_sb[0:NST, bcol:bcol + 1],
                          inc=("vc", 1))
                ctx.wait("vector", "sc")
                for ft in range(16):
                    ctx.v("tensor_mul", dubuf[ft][:, 0:QM], dbuf[ft][:, 0:QM],
                          ubuf[ft][:, 0:QM], inc=("vc", 1) if ft == 15 else None)
                dA = scan_sc[:, 0:QM]
                dxB = scan_sc[:, QM:2 * QM]
                hsc = scan_sc[:, 2 * QM:3 * QM]
                for n in range(NST):
                    bb = ps[2 + (n % 2)]
                    cb = ps[6 + (n % 2)]
                    ctx.wait("tensor", "vc")
                    ctx.mm(bb[:, 0:QM], k16[0:NST, n * 128:(n + 1) * 128],
                           btct[0:NST, 0:QM], inc=("tc", 1))
                    ctx.mm(cb[:, 0:QM], k16[0:NST, n * 128:(n + 1) * 128],
                           btct[0:NST, QM:2 * QM], inc=("tc", 1))
                    ctx.wait("vector", "tc")
                    for et in range(NST):
                        ctx.s("activation", dA, dbuf[et][:, 0:QM], AF.Exp,
                              scale=a_sb[:, et * NST + n:et * NST + n + 1],
                              inc=("sc", 1))
                        ctx.wait("vector", "sc")
                        ctx.v("tensor_mul", dxB, dubuf[et][:, 0:QM], bb[:, 0:QM])
                        ctx.v("tensor_tensor_scan", hsc, dA, dxB,
                              hstate[:, et * NST + n:et * NST + n + 1],
                              ALU.mult, ALU.add)
                        if n == 0:
                            ctx.v("tensor_mul", ybuf[et], hsc,
                                  cb[:, 0:QM])
                        else:
                            ctx.v("tensor_mul", dxB, hsc, cb[:, 0:QM])
                            ctx.v("tensor_add", ybuf[et],
                                  ybuf[et], dxB)
                        ctx.v("tensor_copy",
                              hstate[:, et * NST + n:et * NST + n + 1],
                              hsc[:, QM - 1:QM], inc=("vc", 1))
                        ctx.wait("scalar", "vc")
                if _DBG_STOP == 28:
                    ctx.wait("sync", "vc")
                    ctx.dma("sync", out_t[0:128, base:base + NST * NST],
                            hstate[:, 0:NST * NST], sem="dG")
                for ft in range(16):
                    ctx.v("scalar_tensor_tensor", ybuf[ft],
                          ubuf[ft][:, 0:QM], dskip_sb[:, ft:ft + 1],
                          ybuf[ft], ALU.mult, ALU.add,
                          inc=("vc", 1) if ft == 15 else None)
                if _DBG_STOP == 25:
                    ctx.wait("sync", "vc")
                    for j in range(8):
                        ctx.dma("sync", out_t[j * 128:(j + 1) * 128, base:base + QM],
                                ybuf[j], sem="dG")
                for dt in range(8):
                    par = dt % 2
                    ctx.wait("sync", *wbig_free[par])
                    ctx.dma("sync", wbig[par][:, 0:2048], dram["outwP"][dt])
                    bank = ps[dt % 2]
                    ctx.wait("tensor", "dW")
                    ctx.wait("tensor", "vc")
                    for kt in range(16):
                        ctx.mm(bank[:, 0:QM],
                               wbig[par][:, kt * 128:(kt + 1) * 128],
                               ybuf[kt], start=(kt == 0), stop=(kt == 15),
                               inc=("tc", 1) if kt == 15 else None)
                    wbig_free[par] = ("tc", ctx.cnt["tc"])
                    ctx.wait("vector", "tc")
                    ctx.v("tensor_scalar", dbuf[dt][:, 0:QM], bank[:, 0:QM],
                          outb_sb[:, dt:dt + 1], modcol(mod0, 2, dt, b),
                          ALU.add, ALU.mult)
                    ctx.v("tensor_add", dbuf[dt][:, 0:QM], dbuf[dt][:, 0:QM],
                          xbuf[dt][:, 1:1 + QM], inc=("vc", 1))
                    ctx.wait("sync", "vc")
                    ctx.dma("sync", dram["xres"][dt * 128:(dt + 1) * 128,
                                                 base:base + QM],
                            dbuf[dt][:, 0:QM], sem="dG")
                wconv_free[0] = ("tc", ctx.cnt["tc"])
                wconv_free[1] = ("tc", ctx.cnt["tc"])
        A.reset(m_ph2)

        def finish_from(src):
            ctx.wait("sync", "dG")
            ctx.wait("sync", "vc")
            for j in range(8):
                for c in range(0, T, 2048):
                    ctx.dma("sync", out_t[j * 128:(j + 1) * 128, c:c + 2048],
                            src[j * 128:(j + 1) * 128, c:c + 2048])
            ctx.wait("gpsimd", "dW")
            ctx.wait("gpsimd", "dG")

        if 21 <= _DBG_STOP <= 28:
            ctx.wait("gpsimd", "dW")
            ctx.wait("gpsimd", "dG")
            return
        if _DBG_STOP == 2:
            finish_from(dram["xres"])
            return

        m_ph3 = A.mark()
        SC3 = 1024
        xt3 = [al(SC3, f"p3x{j}") for j in range(8)]
        ht3 = [al(SC3, f"p3h{j}") for j in range(8)]
        dnacc = xt3
        gate_sb = [al(512, f"p3g{m}") for m in range(4)]
        sig_sb = [al(512, f"p3s{m}") for m in range(4)]
        act_sb = [al(512, f"p3a{m}") for m in range(4)]
        usb_t = al(512, "p3u"); glu_t = al(512, "p3glu")
        scbc_sb = al(512, "p3scbc")
        lgt = al(128, "p3lg"); oh1 = al(128, "p3oh1"); oh2 = al(128, "p3oh2")
        msk = al(128, "p3msk"); scrt = al(128, "p3scr")
        tkc = al(48, "p3tk")
        scoresT = al(SC3, "p3scT")
        rtr_sb = al(128, "p3rtr"); gub_sb = al(128, "p3gub")
        dnb_sb = al(1024, "p3dnb")
        wgu_r = al(8192, "p3wgu")
        wdn_r = [al(4096, "p3wdn0"), al(4096, "p3wdn1")]
        xtmp3 = al(2048, "p3xtmp")
        res3 = gate_sb + sig_sb

        ctx.wait("sync", "dG")
        ctx.wait("sync", "tc")
        ctx.wait("sync", "sc")
        ctx.wait("sync", "vc")
        ctx.dma("sync", rtr_sb[:, 0:128].rearrange("p (a e) -> p a e", e=NEXP),
                dram["rtr"].rearrange("(a p) e -> p a e", p=128), slow=True)
        ctx.dma("sync", gub_sb[:, 0:128].rearrange("p (e c) -> p e c", c=8),
                dram["gub"].rearrange("e (c p) -> p e c", p=128), slow=True)
        ctx.dma("sync", dnb_sb[0:NEXP, 0:1024], dram["dnb"])

        wgu_free = ("tc", 0)
        wdn_free = [("tc", 0), ("tc", 0)]

        def p3_wload(e):
            nonlocal wgu_free
            ctx.wait("sync", *wgu_free)
            ctx.dma("sync", wgu_r[:, 0:8192].rearrange("p (kt m) -> p kt m", m=1024),
                    dram["guw"][e].rearrange("(kt p) m -> p kt m", p=128))
            par = e % 2
            ctx.wait("sync", *wdn_free[par])
            ctx.dma("sync", wdn_r[par][:, 0:4096].rearrange("p (it m) -> p it m", m=1024),
                    dram["dnw"][e].rearrange("(it p) m -> p it m", p=128))

        for sc in range(4):
            b3 = sc // 2
            base3 = sc * SC3
            ctx.wait("sync", "vc")
            ctx.wait("sync", "dG")
            for j in range(8):
                ctx.dma("sync", xt3[j][:, 0:SC3],
                        dram["xres"][j * 128:(j + 1) * 128, base3:base3 + SC3])
            if sc == 0:
                p3_wload(0)
            ctx.wait("scalar", "dW")
            ctx.wait("vector", "dW")
            ctx.wait("scalar", "vc")
            ctx.wait("scalar", "tc")
            ctx.wait("tensor", "vc")
            ctx.wait("vector", "tc")
            ln_mod([x[:, 0:SC3] for x in xt3], [h[:, 0:SC3] for h in ht3],
                   [h[:, 0:SC3] for h in ht3], SC3, mod0, 4, 3, b3)
            for tt in range(8):
                bank = ps[tt % 2]
                ctx.wait("tensor", "vc")
                for kt in range(8):
                    ctx.mm(bank[:, 0:NEXP], ht3[kt][:, tt * 128:(tt + 1) * 128],
                           rtr_sb[:, kt * 16:(kt + 1) * 16],
                           start=(kt == 0), stop=(kt == 7),
                           inc=("tc", 1) if kt == 7 else None)
                ctx.wait("vector", "tc")
                ctx.v("tensor_copy", lgt[:, tt * 16:(tt + 1) * 16],
                      bank[:, 0:NEXP], inc=("vc", 1))
            for tt in range(8):
                ctx.v("reduce_max", tkc[:, tt:tt + 1],
                      lgt[:, tt * 16:(tt + 1) * 16], AX.X)
            for tt in range(8):
                ctx.v("tensor_scalar", oh1[:, tt * 16:(tt + 1) * 16],
                      lgt[:, tt * 16:(tt + 1) * 16], tkc[:, tt:tt + 1], None,
                      ALU.is_equal)
            for tt in range(8):
                ctx.v("scalar_tensor_tensor", msk[:, tt * 16:(tt + 1) * 16],
                      oh1[:, tt * 16:(tt + 1) * 16], -1e30,
                      lgt[:, tt * 16:(tt + 1) * 16], ALU.mult, ALU.add)
            for tt in range(8):
                ctx.v("reduce_max", tkc[:, 8 + tt:9 + tt],
                      msk[:, tt * 16:(tt + 1) * 16], AX.X)
            for tt in range(8):
                ctx.v("tensor_scalar", oh2[:, tt * 16:(tt + 1) * 16],
                      lgt[:, tt * 16:(tt + 1) * 16], tkc[:, 8 + tt:9 + tt], None,
                      ALU.is_equal)
            ctx.v("tensor_sub", tkc[:, 16:24], tkc[:, 8:16], tkc[:, 0:8],
                  inc=("vc", 1))
            ctx.wait("scalar", "vc")
            ctx.s("activation", tkc[:, 24:32], tkc[:, 16:24], AF.Exp, inc=("sc", 1))
            ctx.wait("vector", "sc")
            ctx.v("tensor_scalar_add", tkc[:, 32:40], tkc[:, 24:32], 1.0)
            ctx.v("tensor_copy", scrt[:, 0:128], oh1[:, 0:128])
            ctx.v("reciprocal", tkc[:, 40:48], tkc[:, 32:40])
            ctx.v("tensor_copy", scrt[:, 0:128], oh2[:, 0:128])
            ctx.v("tensor_copy", msk[:, 0:128], oh1[:, 0:128])
            ctx.v("tensor_mul", tkc[:, 32:40], tkc[:, 24:32], tkc[:, 40:48])
            for tt in range(8):
                ctx.v("tensor_scalar", msk[:, tt * 16:(tt + 1) * 16],
                      oh1[:, tt * 16:(tt + 1) * 16], tkc[:, 40 + tt:41 + tt], None,
                      ALU.mult)
            for tt in range(8):
                ctx.v("scalar_tensor_tensor", scrt[:, tt * 16:(tt + 1) * 16],
                      oh2[:, tt * 16:(tt + 1) * 16], tkc[:, 32 + tt:33 + tt],
                      msk[:, tt * 16:(tt + 1) * 16], ALU.mult, ALU.add,
                      inc=("vc", 1) if tt == 7 else None)
            for tt in range(8):
                bank = ps[2 + tt % 2]
                ctx.wait("tensor", "vc")
                ctx.tr(bank[0:16, 0:128], scrt[:, tt * 16:(tt + 1) * 16], identSB,
                       inc=("tc", 1))
                ctx.wait("vector", "tc")
                ctx.v("tensor_copy", scoresT[0:16, tt * 128:(tt + 1) * 128],
                      bank[0:16, 0:128], inc=("vc", 1))
            for e in range(NEXP):
                par = e % 2
                for sub in range(2):
                    t0 = sub * 512
                    ctx.wait("tensor", "vc")
                    ctx.wait("tensor", "dW")
                    for mt in range(4):
                        for kt in range(8):
                            ctx.mm(ps[mt][:, 0:512],
                                   wgu_r[:, kt * 1024 + mt * 128:
                                         kt * 1024 + (mt + 1) * 128],
                                   ht3[kt][:, t0:t0 + 512],
                                   start=(kt == 0), stop=(kt == 7),
                                   inc=("tc", 1) if kt == 7 else None)
                    ctx.mm(ps[4][:, 0:512], k16[0:16, e * 128:(e + 1) * 128],
                           scoresT[0:16, t0:t0 + 512], inc=("tc", 1))
                    ctx.wait("vector", "tc")
                    for mt in range(4):
                        ctx.v("tensor_scalar", gate_sb[mt], ps[mt][:, 0:512],
                              gub_sb[:, e * 8 + mt:e * 8 + mt + 1], 7.0,
                              ALU.add, ALU.min, inc=("vc", 1))
                    ctx.v("tensor_copy", scbc_sb[:, 0:512], ps[4][:, 0:512],
                          inc=("vc", 1))
                    ctx.wait("scalar", "vc")
                    for mt in range(4):
                        ctx.s("activation", sig_sb[mt], gate_sb[mt], AF.Sigmoid,
                              scale=1.702, inc=("sc", 1))
                    ctx.wait("tensor", "vc")
                    for mt in range(4):
                        for kt in range(8):
                            ctx.mm(ps[mt][:, 0:512],
                                   wgu_r[:, kt * 1024 + 512 + mt * 128:
                                         kt * 1024 + 512 + (mt + 1) * 128],
                                   ht3[kt][:, t0:t0 + 512],
                                   start=(kt == 0), stop=(kt == 7),
                                   inc=("tc", 1) if kt == 7 else None)
                    if sub == 1:
                        wgu_free = ("tc", ctx.cnt["tc"])
                        p3_wload((e + 1) % NEXP)
                    ctx.wait("vector", "tc")
                    ctx.wait("vector", "sc")
                    for mt in range(4):
                        ctx.v("tensor_scalar", usb_t[:, 0:512], ps[mt][:, 0:512],
                              gub_sb[:, e * 8 + 4 + mt:e * 8 + 5 + mt], 7.0,
                              ALU.add, ALU.min)
                        ctx.v("tensor_scalar", usb_t[:, 0:512], usb_t[:, 0:512],
                              -7.0, 1.0, ALU.max, ALU.add)
                        ctx.v("tensor_mul", glu_t[:, 0:512], gate_sb[mt],
                              sig_sb[mt])
                        ctx.v("tensor_mul", act_sb[mt], glu_t[:, 0:512],
                              usb_t[:, 0:512])
                        ctx.v("tensor_mul", act_sb[mt], act_sb[mt],
                              scbc_sb[:, 0:512], inc=("vc", 1))
                    ctx.wait("tensor", "vc")
                    for mt in range(8):
                        for it in range(4):
                            ctx.mm(ps[mt][:, 0:512],
                                   wdn_r[par][:, it * 1024 + mt * 128:
                                              it * 1024 + (mt + 1) * 128],
                                   act_sb[it],
                                   start=(it == 0), stop=(it == 3),
                                   inc=("tc", 1) if it == 3 else None)
                    if sub == 1:
                        wdn_free[par] = ("tc", ctx.cnt["tc"])
                    ctx.wait("vector", "tc")
                    for mt in range(8):
                        dst = dnacc[mt][:, t0:t0 + 512]
                        if e == 0:
                            ctx.v("tensor_copy", dst, ps[mt][:, 0:512],
                                  inc=("vc", 1))
                        else:
                            ctx.v("tensor_add", dst, dst, ps[mt][:, 0:512],
                                  inc=("vc", 1))
            for sub in range(2):
                t0 = sub * 512
                ctx.wait("tensor", "vc")
                for mt in range(8):
                    ctx.mm(ps[mt][:, 0:512], dnb_sb[0:16, mt * 128:(mt + 1) * 128],
                           scoresT[0:16, t0:t0 + 512], inc=("tc", 1))
                for half in range(2):
                    ctx.wait("sync", "vc")
                    for i in range(4):
                        dt = half * 4 + i
                        ctx.dma("sync", xtmp3[:, i * 512:(i + 1) * 512],
                                dram["xres"][dt * 128:(dt + 1) * 128,
                                             base3 + t0:base3 + t0 + 512])
                    ctx.wait("vector", "dW")
                    ctx.wait("vector", "tc")
                    for i in range(4):
                        dt = half * 4 + i
                        gbAP = modcol(mod0, 5, dt, b3)
                        ctx.v("scalar_tensor_tensor", res3[dt],
                              dnacc[dt][:, t0:t0 + 512], gbAP,
                              xtmp3[:, i * 512:(i + 1) * 512], ALU.mult, ALU.add)
                        ctx.v("scalar_tensor_tensor", res3[dt], ps[dt][:, 0:512],
                              gbAP, res3[dt], ALU.mult, ALU.add, inc=("vc", 1))
                    ctx.wait("sync", "vc")
                    for i in range(4):
                        dt = half * 4 + i
                        ctx.dma("sync",
                                dram["xres"][dt * 128:(dt + 1) * 128,
                                             base3 + t0:base3 + t0 + 512],
                                res3[dt], sem="dG")
        A.reset(m_ph3)

        if _DBG_STOP == 3:
            finish_from(dram["xres"])
            return

        m_ph4 = A.mark()
        w4 = [al(8192, "p4w0"), al(8192, "p4w1")]
        ev4 = [al(512, f"p4e{j}") for j in range(8)]
        ropc = al(2048, "p4cos"); rops = al(2048, "p4sin")
        mask4 = al(4 * 512, "p4mask")
        colj5 = al(512, "p4colj")
        rowo = al(8, "p4rowo")
        pswap = al(128, "p4swap")
        swc = al(1, "p4swc")
        ctx.wait("sync", "dG")
        ctx.wait("sync", "tc")
        ctx.wait("sync", "sc")
        ctx.wait("sync", "vc")
        ctx.dma("sync", ropc[:, 0:L], dram["ropec"])
        ctx.dma("sync", rops[:, 0:L], dram["ropes2"])
        ctx.dma("sync", swc[:, 0:1], dram["pswapc"])
        ctx.wait("gpsimd", "vc")
        ctx.wait("gpsimd", "tc")
        ctx.wait("gpsimd", "sc")
        ctx.wait("gpsimd", "dW")
        ctx.g("iota", colj5[:, 0:512], [[1, 512]], base=0, channel_multiplier=0,
              allow_small_or_imprecise_dtypes=True, inc=("gc", 1))
        ctx.wait("vector", "gc")
        ctx.wait("vector", "dW")
        for dd in range(4):
            ctx.v("tensor_scalar_add", rowo[:, dd:dd + 1], rowi[:, 0:1],
                  float(dd * 128 - 1))
        ctx.v("tensor_scalar", pswap[:, 0:128], colj[:, 0:128], swc[:, 0:1],
              None, ALU.is_equal)
        for dd in range(4):
            ctx.v("tensor_scalar", mask4[:, dd * 512:(dd + 1) * 512],
                  colj5[:, 0:512], rowo[:, dd:dd + 1], None, ALU.is_gt,
                  inc=("vc", 1) if dd == 3 else None)

        w4free = [("tc", 0), ("tc", 0)]
        wnext4 = [0]

        def slab_load(srcap):
            wi = wnext4[0]
            wnext4[0] ^= 1
            ctx.wait("sync", *w4free[wi])
            ctx.dma("sync", w4[wi][:, 0:8192].rearrange("p (kt m) -> p kt m",
                                                        m=1024),
                    srcap.rearrange("(kt p) m -> p kt m", p=128))
            return wi

        def rope_tiles(tiles, c0):
            cs = ropc[:, c0:c0 + 512]
            sn = rops[:, c0:c0 + 512]
            for mt in range(8):
                bank = ps[2 + mt % 2]
                ctx.wait("tensor", "vc")
                ctx.mm(bank[:, 0:512], pswap[:, 0:128], tiles[mt],
                       inc=("tc", 1))
                ctx.wait("vector", "tc")
                ctx.v("tensor_mul", rtmp4[:, 0:512], tiles[mt], cs)
                ctx.v("tensor_mul", rtmp4b[:, 0:512], bank[:, 0:512], sn)
                ctx.v("tensor_add", tiles[mt], rtmp4[:, 0:512],
                      rtmp4b[:, 0:512], inc=("vc", 1))

        for b in range(B_C):
            m4b = A.mark()
            xt4 = [al(512, f"p4x{j}") for j in range(8)]
            ht4 = [al(512, f"p4h{j}") for j in range(8)]
            rtmp4 = al(512, "p4rt"); rtmp4b = al(512, "p4rtb")
            for ch in range(4):
                base4 = b * L + ch * 512
                wq_i = slab_load(dram["wqp"])
                wk_i = slab_load(dram["wkeff"])
                ctx.wait("sync", "vc")
                ctx.wait("sync", "dG")
                for j in range(8):
                    ctx.dma("sync", xt4[j][:, 0:512],
                            dram["xres"][j * 128:(j + 1) * 128, base4:base4 + 512])
                ctx.wait("scalar", "dW")
                ctx.wait("vector", "dW")
                ctx.wait("scalar", "vc")
                ctx.wait("scalar", "tc")
                ctx.wait("tensor", "vc")
                ctx.wait("vector", "tc")
                ln_mod([x[:, 0:512] for x in xt4], [h[:, 0:512] for h in ht4],
                       [h[:, 0:512] for h in ht4], 512, mod1, 1, 0, b)
                for pi, (wi_or_src, outnm) in enumerate(
                        [(wq_i, "qTs"), (wk_i, "kTs"), (None, "wveff")]):
                    if pi == 2:
                        wi = slab_load(dram["wveff"])
                    else:
                        wi = wi_or_src
                    ctx.wait("tensor", "dW")
                    if pi < 2:
                        for mt in range(8):
                            bank = ps[mt % 2]
                            ctx.wait("tensor", "vc")
                            for kt in range(8):
                                ctx.mm(bank[:, 0:512],
                                       w4[wi][:, kt * 1024 + mt * 128:
                                              kt * 1024 + (mt + 1) * 128],
                                       ht4[kt][:, 0:512],
                                       start=(kt == 0), stop=(kt == 7),
                                       inc=("tc", 1) if kt == 7 else None)
                            ctx.wait("vector", "tc")
                            ctx.wait("vector", "dG")
                            ctx.v("tensor_copy", ev4[mt], bank[:, 0:512],
                                  inc=("vc", 1))
                        w4free[wi] = ("tc", ctx.cnt["tc"])
                        if _DBG_STOP == 44 and b == 0 and ch == 0 and pi == 0:
                            ctx.wait("sync", "vc")
                            for mt in range(8):
                                ctx.dma("sync",
                                        out_t[mt * 128:(mt + 1) * 128, 0:512],
                                        ev4[mt], sem="dG")
                            for j in range(8):
                                ctx.dma("sync",
                                        out_t[j * 128:(j + 1) * 128, 512:1024],
                                        ht4[j][:, 0:512], sem="dG")
                                ctx.dma("sync",
                                        out_t[j * 128:(j + 1) * 128, 1024:1536],
                                        xt4[j][:, 0:512], sem="dG")
                            ctx.dma("sync", out_t[0:128, 1536:2048],
                                    ropc[:, 0:512], sem="dG")
                            ctx.wait("gpsimd", "dW")
                            ctx.wait("gpsimd", "dG")
                            return
                        rope_tiles(ev4, ch * 512)
                        ctx.wait("sync", "vc")
                        for mt in range(8):
                            ctx.dma("sync",
                                    dram[outnm][mt * 128:(mt + 1) * 128,
                                                ch * 512:(ch + 1) * 512],
                                    ev4[mt], sem="dG")
                    else:
                        for tb in range(4):
                            for mh in range(2):
                                bank = ps[2 + (tb * 2 + mh) % 2]
                                ctx.wait("tensor", "vc")
                                for kt in range(8):
                                    ctx.mm(bank[:, 0:512],
                                           ht4[kt][:, tb * 128:(tb + 1) * 128],
                                           w4[wi][:, kt * 1024 + mh * 512:
                                                  kt * 1024 + (mh + 1) * 512],
                                           start=(kt == 0), stop=(kt == 7),
                                           inc=("tc", 1) if kt == 7 else None)
                                ctx.wait("vector", "tc")
                                ctx.wait("vector", "dG")
                                ctx.v("tensor_copy", ev4[tb * 2 + mh],
                                      bank[:, 0:512], inc=("vc", 1))
                        w4free[wi] = ("tc", ctx.cnt["tc"])
                        ctx.wait("sync", "vc")
                        for tb in range(4):
                            for mh in range(2):
                                ctx.dma("sync",
                                        dram["vtok"][ch * 512 + tb * 128:
                                                     ch * 512 + (tb + 1) * 128,
                                                     mh * 512:(mh + 1) * 512],
                                        ev4[tb * 2 + mh], sem="dG")
            if _DBG_STOP == 41 and b == 0:
                ctx.wait("sync", "dG")
                for j in range(8):
                    ctx.dma("sync", out_t[j * 128:(j + 1) * 128, 0:L],
                            dram["qTs"][j * 128:(j + 1) * 128, :], sem="dG")
                    ctx.dma("sync", out_t[j * 128:(j + 1) * 128, L:2 * L],
                            dram["kTs"][j * 128:(j + 1) * 128, :], sem="dG")
                ctx.wait("gpsimd", "dW")
                ctx.wait("gpsimd", "dG")
                return
            if _DBG_STOP == 43 and b == 0:
                ctx.wait("sync", "dG")
                ctx.dma("sync", out_t[:, 0:L],
                        dram["vtok"].rearrange("t d -> d t"), sem="dG",
                        slow=True)
                ctx.wait("gpsimd", "dW")
                ctx.wait("gpsimd", "dG")
                return
            A.reset(m4b)
            qh = al(2048, "p4qh"); kh = al(2048, "p4kh"); vh = al(1024, "p4vh")
            probsT = [al(512, f"p4p{k}") for k in range(16)]
            ysb = al(512, "p4y"); bcsb = al(512, "p4bc")
            ctx.wait("sync", "dG")
            for h in range(NH):
                ctx.wait("sync", "tc")
                ctx.wait("sync", "sc")
                ctx.dma("sync", qh[0:64, 0:L], dram["qTs"][h * 64:(h + 1) * 64, :])
                ctx.dma("sync", kh[0:64, 0:L], dram["kTs"][h * 64:(h + 1) * 64, :])
                ctx.dma("sync",
                        vh[:, 0:1024].rearrange("p (kt m) -> p kt m", m=64),
                        dram["vtok"].rearrange("(kt p) m -> p kt m",
                                               p=128)[:, :, h * 64:(h + 1) * 64])
                for qc in range(4):
                    nkt = 4 * (qc + 1)
                    for kt in range(nkt):
                        bank = ps[kt % 2]
                        ctx.wait("tensor", "sc")
                        ctx.wait("tensor", "vc")
                        ctx.wait("tensor", "dW")
                        ctx.mm(bank[:, 0:512], kh[0:64, kt * 128:(kt + 1) * 128],
                               qh[0:64, qc * 512:(qc + 1) * 512], inc=("tc", 1))
                        ctx.wait("scalar", "tc")
                        ctx.s("activation", probsT[kt], bank[:, 0:512], AF.Exp,
                              scale=0.125, inc=("sc", 1))
                    ctx.wait("vector", "sc")
                    ctx.wait("vector", "tc")
                    for dd in range(4):
                        kt = qc * 4 + dd
                        ctx.v("tensor_mul", probsT[kt], probsT[kt],
                              mask4[:, dd * 512:(dd + 1) * 512],
                              inc=("vc", 1) if dd == 3 else None)
                    ctx.v("engine_nop", inc=("vc", 1))
                    ctx.wait("tensor", "vc")
                    ctx.wait("tensor", "sc")
                    for kt in range(nkt):
                        ctx.mm(ps[2][0:1, 0:512], ones[:, 0:1], probsT[kt],
                               start=(kt == 0), stop=(kt == nkt - 1),
                               inc=("tc", 1) if kt == nkt - 1 else None)
                    for kt in range(nkt):
                        ctx.mm(ps[3][0:64, 0:512], vh[:, kt * 64:(kt + 1) * 64],
                               probsT[kt],
                               start=(kt == 0), stop=(kt == nkt - 1),
                               inc=("tc", 1) if kt == nkt - 1 else None)
                    ctx.wait("vector", "tc")
                    ctx.wait("vector", "dG")
                    rec = rows_sc[0:1, 0:512]
                    ctx.v("reciprocal", rec, ps[2][0:1, 0:512], inc=("vc", 1))
                    ctx.wait("tensor", "vc")
                    ctx.mm(ps[4][0:64, 0:512], ones[0:1, 0:64], rec,
                           inc=("tc", 1))
                    ctx.wait("vector", "tc")
                    ctx.v("tensor_copy", bcsb[0:64, 0:512], ps[4][0:64, 0:512])
                    ctx.v("tensor_copy", ysb[0:64, 0:512], ps[3][0:64, 0:512])
                    ctx.v("engine_nop")
                    ctx.v("engine_nop")
                    ctx.v("tensor_mul", ysb[0:64, 0:512], ysb[0:64, 0:512],
                          bcsb[0:64, 0:512], inc=("vc", 1))
                    ctx.wait("sync", "vc")
                    ctx.dma("sync",
                            dram["yTs"][h * 64:(h + 1) * 64,
                                        qc * 512:(qc + 1) * 512],
                            ysb[0:64, 0:512], sem="dG")
            if _DBG_STOP == 42 and b == 0:
                ctx.wait("sync", "dG")
                for j in range(8):
                    ctx.dma("sync", out_t[j * 128:(j + 1) * 128, 0:L],
                            dram["yTs"][j * 128:(j + 1) * 128, :], sem="dG")
                ctx.wait("gpsimd", "dW")
                ctx.wait("gpsimd", "dG")
                return
            A.reset(m4b)
            yt4 = [al(512, f"p4yt{j}") for j in range(8)]
            xr4 = [al(512, f"p4xr{j}") for j in range(8)]
            ctx.wait("sync", "dG")
            for ch in range(4):
                base4 = b * L + ch * 512
                wi = slab_load(dram["wproj"])
                ctx.wait("sync", "tc")
                ctx.wait("sync", "vc")
                for j in range(8):
                    ctx.dma("sync", yt4[j][:, 0:512],
                            dram["yTs"][j * 128:(j + 1) * 128,
                                        ch * 512:(ch + 1) * 512])
                    ctx.dma("sync", xr4[j][:, 0:512],
                            dram["xres"][j * 128:(j + 1) * 128,
                                         base4:base4 + 512])
                ctx.wait("tensor", "dW")
                for mt in range(8):
                    bank = ps[mt % 2]
                    ctx.wait("tensor", "vc")
                    for kt in range(8):
                        ctx.mm(bank[:, 0:512],
                               w4[wi][:, kt * 1024 + mt * 128:
                                      kt * 1024 + (mt + 1) * 128],
                               yt4[kt][:, 0:512],
                               start=(kt == 0), stop=(kt == 7),
                               inc=("tc", 1) if kt == 7 else None)
                    ctx.wait("vector", "tc")
                    ctx.wait("vector", "dW")
                    ctx.wait("vector", "dG")
                    ctx.v("scalar_tensor_tensor", ev4[mt], bank[:, 0:512],
                          modcol(mod1, 2, mt, b), xr4[mt][:, 0:512],
                          ALU.mult, ALU.add, inc=("vc", 1))
                    ctx.wait("sync", "vc")
                    ctx.dma("sync",
                            dram["xres"][mt * 128:(mt + 1) * 128,
                                         base4:base4 + 512],
                            ev4[mt], sem="dG")
                w4free[wi] = ("tc", ctx.cnt["tc"])
            A.reset(m4b)
        A.reset(m_ph4)

        if _DBG_STOP == 4:
            finish_from(dram["xres"])
            return

        m_ph5 = A.mark()
        w5 = [al(8192, "p5w0"), al(8192, "p5w1")]
        xt5 = [al(512, f"p5x{j}") for j in range(8)]
        ht5 = [al(512, f"p5h{j}") for j in range(8)]
        ev5 = [al(512, f"p5e{j}") for j in range(8)]
        w5free = [("tc", 0), ("tc", 0)]
        ctx.wait("sync", "dG")
        ctx.wait("sync", "tc")
        ctx.wait("sync", "sc")
        ctx.wait("sync", "vc")
        for ch in range(8):
            b = ch // 4
            base5 = ch * 512
            wi = ch % 2
            ctx.wait("sync", *w5free[wi])
            ctx.dma("sync", w5[wi][:, 0:8192].rearrange("p (kt m) -> p kt m",
                                                        m=1024),
                    dram["wcomb"].rearrange("(kt p) m -> p kt m", p=128))
            ctx.wait("sync", "vc")
            ctx.wait("sync", "dG")
            for j in range(8):
                ctx.dma("sync", xt5[j][:, 0:512],
                        dram["xres"][j * 128:(j + 1) * 128, base5:base5 + 512])
            ctx.wait("scalar", "dW")
            ctx.wait("vector", "dW")
            ctx.wait("scalar", "vc")
            ctx.wait("scalar", "tc")
            ctx.wait("tensor", "vc")
            ctx.wait("vector", "tc")
            ln_mod([x[:, 0:512] for x in xt5], [h[:, 0:512] for h in ht5],
                   [h[:, 0:512] for h in ht5], 512, mod1, 4, 3, b)
            ctx.wait("tensor", "dW")
            for mt in range(8):
                bank = ps[mt % 2]
                ctx.wait("tensor", "vc")
                for kt in range(8):
                    ctx.mm(bank[:, 0:512],
                           w5[wi][:, kt * 1024 + mt * 128:
                                  kt * 1024 + (mt + 1) * 128],
                           ht5[kt][:, 0:512],
                           start=(kt == 0), stop=(kt == 7),
                           inc=("tc", 1) if kt == 7 else None)
                ctx.wait("vector", "tc")
                ctx.wait("vector", "dG")
                ctx.v("tensor_scalar", ev5[mt], bank[:, 0:512],
                      fb_sb[:, mt:mt + 1], modcol(mod1, 5, mt, b),
                      ALU.add, ALU.mult)
                ctx.v("tensor_add", ev5[mt], ev5[mt], xt5[mt][:, 0:512],
                      inc=("vc", 1))
                ctx.wait("sync", "vc")
                ctx.dma("sync", out_t[mt * 128:(mt + 1) * 128,
                                      base5:base5 + 512],
                        ev5[mt], sem="dG")
            w5free[wi] = ("tc", ctx.cnt["tc"])
        ctx.wait("gpsimd", "dW")
        ctx.wait("gpsimd", "dG")

    @block.sync
    def _(eng):
        emit_all(Ctx(nc, "sync", eng, sems))

    @block.vector
    def _(eng):
        emit_all(Ctx(nc, "vector", eng, sems))

    @block.scalar
    def _(eng):
        emit_all(Ctx(nc, "scalar", eng, sems))

    @block.tensor
    def _(eng):
        emit_all(Ctx(nc, "tensor", eng, sems))

    @block.gpsimd
    def _(eng):
        emit_all(Ctx(nc, "gpsimd", eng, sems))

    es.close()
    return nc


def _pack_lhsT(wT, mt=128):
    K, M = wT.shape
    return np.ascontiguousarray(
        wT.reshape(K // 128, 128, M // mt, mt).transpose(2, 1, 0, 3)
        .reshape(M // mt, 128, (K // 128) * mt))


def _prep_inputs(full):
    f32 = np.float32
    sh = {}
    convw = full["L0_attn_conv_w"].transpose(2, 1, 0)
    sh["convwP"] = np.ascontiguousarray(
        convw.reshape(3, 8, 128, 16, 128).transpose(3, 2, 0, 1, 4)
        .reshape(16, 128, 3072))
    sh["convb"] = np.ascontiguousarray(full["L0_attn_conv_b"].reshape(E, 1))
    sh["inwP"] = _pack_lhsT(full["L0_attn_in_w"].T)
    sh["inb"] = np.ascontiguousarray(full["L0_attn_in_b"].reshape(E, 1))
    sh["dtwP"] = _pack_lhsT(full["L0_attn_dt_w"].T)
    sh["dtb"] = np.ascontiguousarray(full["L0_attn_dt_b"].reshape(E, 1))
    sh["bpwP"] = _pack_lhsT(full["L0_attn_Bp_w"].T, mt=16)[0]
    sh["bpb"] = np.ascontiguousarray(full["L0_attn_Bp_b"].reshape(NST, 1))
    sh["cpwP"] = _pack_lhsT(full["L0_attn_Cp_w"].T, mt=16)[0]
    sh["cpb"] = np.ascontiguousarray(full["L0_attn_Cp_b"].reshape(NST, 1))
    sh["alog"] = np.ascontiguousarray(full["L0_attn_A_log"])
    sh["dskip"] = np.ascontiguousarray(full["L0_attn_Dskip"].reshape(E, 1))
    sh["outwP"] = _pack_lhsT(full["L0_attn_out_w"].T)
    sh["outb"] = np.ascontiguousarray(full["L0_attn_out_b"].reshape(D, 1))
    sh["wmod0P"] = _pack_lhsT(full["L0_wmod"])
    sh["rtr"] = np.ascontiguousarray(full["L0_ffn_router_weight"])
    gu = full["L0_ffn_gate_up_proj"]
    sh["guw"] = np.ascontiguousarray(
        np.concatenate([gu[:, :, 0::2], gu[:, :, 1::2]], axis=2))
    gub = full["L0_ffn_gate_up_proj_bias"]
    sh["gub"] = np.ascontiguousarray(
        np.concatenate([gub[:, 0::2], gub[:, 1::2]], axis=1))
    sh["dnw"] = np.ascontiguousarray(full["L0_ffn_down_proj"])
    sh["dnb"] = np.ascontiguousarray(full["L0_ffn_down_proj_bias"])
    sh["wmod1P"] = _pack_lhsT(full["L1_wmod"])
    perm = np.zeros(D, np.int64)
    for h in range(NH):
        for c in range(2):
            for f in range(32):
                perm[h * 64 + c * 32 + f] = h * 64 + 2 * f + c
    sh["wqp"] = np.ascontiguousarray(full["L1_attn_Wq"][:, perm])
    sh["wkT"] = np.ascontiguousarray(full["L1_attn_Wk"].T)
    kd = full["L1_attn_key_decoder"]
    br, bi = kd[..., 0], kd[..., 1]
    mpos = np.zeros((192, D), f32)
    mneg = np.zeros((192, D), f32)
    for f in range(32):
        for r in range(3):
            rr, ri = f * 6 + r * 2 + 0, f * 6 + r * 2 + 1
            for h in range(NH):
                creal, cimag = h * 64 + f, h * 64 + 32 + f
                mpos[rr, creal] = br[f, h, r]
                mpos[ri, cimag] = br[f, h, r]
                mpos[rr, cimag] = bi[f, h, r]
                mneg[ri, creal] = bi[f, h, r]
    sh["mkpos"], sh["mkneg"] = mpos, mneg
    sh["wvT"] = np.ascontiguousarray(full["L1_attn_Wv"].T)
    sh["vdm"] = np.ascontiguousarray(
        full["L1_attn_value_decoder"].transpose(1, 0, 2).reshape(192, D))
    sh["wproj"] = np.ascontiguousarray(full["L1_attn_Wproj"])
    sh["g1rT"] = np.ascontiguousarray(
        full["L1_ffn_G1_R"][0].transpose(1, 2, 0).reshape(4, 1024))
    sh["g2r"] = np.ascontiguousarray(
        full["L1_ffn_G2_R"][:, :, 0, :].transpose(0, 2, 1).reshape(4, 256))
    sh["g1iT"] = np.ascontiguousarray(
        full["L1_ffn_G1_I"][0].transpose(1, 2, 0).reshape(4, 1024))
    sh["g2i"] = np.ascontiguousarray(
        full["L1_ffn_G2_I"][:, :, 0, :].transpose(0, 2, 1).reshape(4, 256))
    sh["fb"] = np.ascontiguousarray(
        np.concatenate([full["L1_ffn_bias_real"], full["L1_ffn_bias_imag"]])
        .reshape(D, 1))
    half = 32
    freq = 1.0 / 10000.0 ** (np.arange(half, dtype=np.float32) / half)
    fr = np.outer(np.arange(L, dtype=np.float32), freq)
    sinT = np.sin(fr).T
    sh["ropec"] = np.ascontiguousarray(np.tile(np.cos(fr).T, (4, 1)))
    sh["ropes2"] = np.ascontiguousarray(
        np.concatenate([-sinT, sinT, -sinT, sinT], axis=0))
    sh["pswapc"] = (np.arange(128, dtype=np.int64) ^ 32).astype(
        np.float32).reshape(128, 1)
    per = []
    x = full["L0_attn_x"]
    for core in range(NCORES):
        pc = {}
        xs = x[core * B_C:(core + 1) * B_C]
        pc["x_t"] = np.ascontiguousarray(xs.reshape(T, D).T)
        pc["c0_t"] = np.ascontiguousarray(full["L0_c"][core * B_C:(core + 1) * B_C].T)
        pc["c1_t"] = np.ascontiguousarray(full["L1_c"][core * B_C:(core + 1) * B_C].T)
        per.append(pc)
    return sh, per


def kernel(**inputs):
    sh, per = _prep_inputs({k: np.asarray(v, np.float32) for k, v in inputs.items()})
    nc = build_program()
    in_maps = [dict(sh, **pc) for pc in per]
    res = run_bass_kernel_spmd(nc, in_maps, core_ids=list(range(NCORES)))
    outs = []
    for core in range(NCORES):
        ot = np.asarray(res.results[core]["out_t"])
        outs.append(np.ascontiguousarray(ot.T.reshape(B_C, L, D)))
    return np.concatenate(outs, axis=0)
